# version 20
# baseline (speedup 1.0000x reference)
import math
import numpy as np
import ml_dtypes
import concourse.bass as bass
import concourse.mybir as mybir
from concourse.bass_utils import run_bass_kernel_spmd

F32 = mybir.dt.float32
BF16 = mybir.dt.bfloat16
I32 = mybir.dt.int32
U8 = mybir.dt.uint8
AF = mybir.ActivationFunctionType
ALU = mybir.AluOpType
AX = mybir.AxisListType.X

D = 1024
S = 2048
H = 8
DH = 128
RC = 16
KC = 8
EPS_LN = 1e-5
EPS_GN = 1e-6
R7_5 = float(np.float32(1.0) / np.float32(7.5))

_STOP_AFTER = "full"
_PARANOID = False
_NHEADS = 8
_HEAD_PARTS = 5

ENGS = ("sync", "vector", "scalar", "tensor", "gpsimd")
SEMS = ("ld", "g", "v", "s", "t", "wa", "wb", "ga", "gb")


class _I:
    __slots__ = ("ctx", "inst")

    def __init__(self, ctx, inst):
        self.ctx = ctx
        self.inst = inst

    def inc(self, sem, n=1):
        self.ctx.cnt[sem] += n
        if self.inst is not None:
            self.inst.then_inc(self.ctx.sems[sem], n)
        return self


class Ctx:

    def __init__(self, nc, cur, sems):
        self.nc = nc
        self.cur = cur
        self.sems = sems
        self.cnt = {k: 0 for k in SEMS}

    def op(self, eng, name, *a, **k):
        inst = None
        if self.cur == eng:
            inst = getattr(getattr(self.nc, eng), name)(*a, **k)
        return _I(self, inst)

    def wait(self, eng, sem, val=None):
        v = self.cnt[sem] if val is None else val
        if v <= 0:
            return
        if self.cur == eng:
            getattr(self.nc, eng).wait_ge(self.sems[sem], v)

    def fence(self):
        for e in ENGS:
            for sm in SEMS:
                self.wait(e, sm)

    def vsync(self, inst):
        inst.inc("v")
        self.wait("vector", "v")

    def ssync(self, inst):
        inst.inc("s")
        self.wait("scalar", "s")


def build_program(stop_after="full"):
    nc = bass.Bass("TRN2", target_bir_lowering=False)

    x_dr = nc.dram_tensor("x", [S, D], F32, kind="ExternalInput").ap()
    w_dr = {nm: nc.dram_tensor(nm, [D, D], F32, kind="ExternalInput").ap()
            for nm in ("wq", "wk", "wv", "wg", "wo", "wf")}
    bias_dr = nc.dram_tensor("bias", [1, D], F32, kind="ExternalInput").ap()
    qsc_dr = nc.dram_tensor("qsc", [H, S], F32, kind="ExternalInput").ap()
    ksc_dr = nc.dram_tensor("ksc", [H, S], F32, kind="ExternalInput").ap()
    m128_dr = nc.dram_tensor("m128", [128, 128], BF16, kind="ExternalInput").ap()
    id_dr = nc.dram_tensor("ident", [128, 128], BF16, kind="ExternalInput").ap()
    out_dr = nc.dram_tensor("out", [S, D], F32, kind="ExternalOutput").ap()
    xsp_dr = nc.dram_tensor("xspill", [128, RC * D], F32, kind="Internal").ap()
    wdt_dr = nc.dram_tensor("wdtspill", [128, KC * D], BF16, kind="Internal").ap()

    x_in = x_dr.rearrange("(rc p) d -> p rc d", p=128)
    out_v = out_dr.rearrange("(rc p) d -> p rc d", p=128)

    O_X = 0
    O_T = 65536
    O_V = 98304
    O_G = 131072
    O_W = 163840
    O_S = 188416
    O_C = 196608
    ARENA_B = 212480

    from contextlib import ExitStack
    ctx_mem = ExitStack()
    arena = ctx_mem.enter_context(nc.sbuf_tensor("arena", [128, ARENA_B], U8))
    pbig = ctx_mem.enter_context(nc.psum_tensor("pbig", [128, 1024], F32))
    psc = ctx_mem.enter_context(nc.psum_tensor("psc", [128, 2048], F32))
    pt0 = ctx_mem.enter_context(nc.psum_tensor("pt0", [128, 128], BF16))
    pt1 = ctx_mem.enter_context(nc.psum_tensor("pt1", [128, 128], BF16))

    A = arena[:]

    def vf32(off, n):
        return A[:, off:off + 4 * n].bitcast(F32)

    def vbf(off, n):
        return A[:, off:off + 2 * n].bitcast(BF16)

    def vi32(off, n):
        return A[:, off:off + 4 * n].bitcast(I32)

    x_sb = vf32(O_X, RC * D).rearrange("p (rc d) -> p rc d", d=D)
    qt_sb = vbf(O_X, H * S).rearrange("p (h n) -> p h n", n=S)
    kt_sb = vbf(O_X + 32768, H * S).rearrange("p (h n) -> p h n", n=S)
    tt_sb = vbf(O_T, KC * S).rearrange("p (k n) -> p k n", n=S)
    zt_sb = tt_sb
    v_sb = vbf(O_V, RC * D).rearrange("p (rc d) -> p rc d", d=D)
    xdt_sb = vbf(O_V, KC * S).rearrange("p (k n) -> p k n", n=S)
    g_sb = vbf(O_G, RC * D).rearrange("p (rc d) -> p rc d", d=D)
    wbf_sb = vbf(O_W, KC * D).rearrange("p (k d) -> p k d", d=D)
    wdt_sb = wbf_sb
    wdt_flat = vbf(O_W, KC * D)
    wst_sb = [vf32(O_W + 16384, D), vf32(O_W + 20480, D)]
    strip_sb = [vbf(O_S, 1024), vbf(O_S + 2048, 1024),
                vbf(O_S + 4096, 1024), vbf(O_S + 6144, 1024)]
    gam_sb = [vf32(O_G, S), vf32(O_G + 8192, S)]
    xrow_sb = [vf32(O_S, D), vf32(O_S + 4096, D)]
    t2row_sb = [vbf(O_S, D), vbf(O_S + 2048, D)]
    wdrow_sb = [vbf(O_S, D), vbf(O_S + 2048, D)]
    ffntmp = vf32(O_S + 4096, D)

    qb0 = vf32(O_G, D)
    qb1 = vf32(O_G + 4096, D)
    qb2 = vf32(O_G + 8192, D)
    qby = vf32(O_G + 12288, D)
    qbi = vi32(O_G + 16384, D)
    xdrow_sb = [vbf(O_G + 20480, D), vbf(O_G + 22528, D)]
    qb3 = vf32(O_G + 25600, D)
    qsm = vf32(O_G + 24576, 128)
    qsm2 = vf32(O_G + 25088, 128)
    q_amax2 = [qsm[:, 0:32], qsm2[:, 0:32]]
    q_scl2 = [qsm[:, 32:64], qsm2[:, 32:64]]
    q_rs2 = [qsm[:, 64:96], qsm2[:, 64:96]]

    bias_half = vf32(O_C, D)
    m128_sb = vbf(O_C + 4096, 128)
    id_sb = vbf(O_C + 4352, 128)
    stat = vf32(O_C + 4608, 144)
    ln_sum = stat[:, 0:16]
    ln_sq = stat[:, 16:32]
    ln_mean = stat[:, 32:48]
    ln_rstd = stat[:, 48:64]
    ln_tsc = stat[:, 64:80]
    gn_sum = stat[:, 80:96]
    gn_sq = stat[:, 96:112]
    gn_mean = stat[:, 112:128]
    gn_rstd = stat[:, 128:144]
    gn_tsc = vf32(O_C + 5184, 16)
    sqscr = vf32(O_C + 5248, D)
    yv_sb = [vf32(O_C + 9344, 128), vf32(O_C + 9856, 128)]
    zrow_sb = [vbf(O_C + 10368, 128), vbf(O_C + 10624, 128)]
    negm_sb = vf32(O_C + 10880, 16)
    ones_row = vbf(O_C + 10944, 128)[0:1, :]
    bias_hi = vbf(208384, D)
    bias_lo = vbf(210432, D)
    yvf_hd = vf32(O_W, D)
    zrowf_hd = vbf(O_W + 4096, D)

    sem_ctx = ExitStack()
    sems = {k: sem_ctx.enter_context(nc.semaphore(k)) for k in SEMS}

    def emit(c: Ctx):
        wq_src_done = []
        wq_tr_done = []
        wo_ev = []
        c.op("sync", "dma_start", out=bias_half,
             in_=bias_dr.partition_broadcast(128)).inc("wa", 16)
        c.op("sync", "dma_start", out=id_sb, in_=id_dr).inc("wb", 16)
        x_ld = []
        for i in range(4):
            c.op("sync", "dma_start", out=x_sb[:, 4 * i:4 * i + 4, :],
                 in_=x_in[:, 4 * i:4 * i + 4, :]).inc("ld", 16)
            x_ld.append(c.cnt["ld"])
        c.op("sync", "dma_start", out=m128_sb, in_=m128_dr).inc("ld", 16)
        c.op("vector", "memset", ones_row, 1.0).inc("v")
        c.wait("vector", "wa", 16)
        c.vsync(c.op("vector", "tensor_copy", bias_hi, bias_half))
        c.vsync(c.op("vector", "tensor_tensor", ffntmp, bias_half, bias_hi,
                     ALU.subtract))
        c.op("vector", "tensor_copy", bias_lo, ffntmp).inc("v")
        c.wait("tensor", "wb", 16)

        def ln_chunk(src3d, rc, gate_ld=None):
            if gate_ld is not None:
                c.wait("scalar", "ld", gate_ld)
            c.wait("scalar", "s")
            c.op("scalar", "activation", sqscr, src3d[:, rc, :],
                 AF.Identity,
                 accum_out=ln_sum[:, rc:rc + 1]).inc("s")
            c.wait("scalar", "s")
            c.op("scalar", "activation", sqscr, src3d[:, rc, :], AF.Square,
                 accum_out=ln_sq[:, rc:rc + 1]).inc("s")

        def ln_tail(eps):
            c.wait("vector", "s")
            c.wait("vector", "v")
            c.vsync(c.op("vector", "tensor_scalar", ln_mean, ln_sum, 1.0 / D,
                         None, ALU.mult))
            c.vsync(c.op("vector", "tensor_scalar", ln_sq, ln_sq, 1.0 / D,
                         None, ALU.mult))
            c.vsync(c.op("vector", "scalar_tensor_tensor", ln_tsc, ln_mean,
                         0.0, ln_mean, ALU.bypass, ALU.mult))
            c.vsync(c.op("vector", "tensor_tensor", ln_sq, ln_sq, ln_tsc,
                         ALU.subtract))
            c.vsync(c.op("vector", "tensor_scalar", ln_sq, ln_sq, eps, None,
                         ALU.add))
            c.vsync(c.op("vector", "reciprocal", ln_tsc, ln_sq))
            c.wait("scalar", "v")
            c.op("scalar", "activation", ln_rstd, ln_tsc, AF.Sqrt).inc("s")
            c.wait("vector", "s")
            c.op("vector", "tensor_scalar", negm_sb[:, 0:16], ln_mean, -1.0,
                 None, ALU.mult).inc("v")

        b32 = lambda ap: ap.rearrange("p (a b) -> p a b", b=32)

        def q_stageA(p, src2d, rstd_c):
            amax, scl, rs = q_amax2[p], q_scl2[p], q_rs2[p]
            c.vsync(c.op("vector", "tensor_reduce", amax.unsqueeze(2),
                         b32(src2d), AX, ALU.max, apply_absolute_value=True))
            if rstd_c is not None:
                c.vsync(c.op("vector", "tensor_scalar", scl, amax, rstd_c,
                             None, ALU.mult))
                c.vsync(c.op("vector", "tensor_scalar", scl, scl, R7_5,
                             1e-30, ALU.mult, ALU.max))
            else:
                c.vsync(c.op("vector", "tensor_scalar", scl, amax, R7_5,
                             1e-30, ALU.mult, ALU.max))
            c.vsync(c.op("vector", "reciprocal", rs, scl))
            rsb = rs.unsqueeze(2).broadcast_to([128, 32, 32])
            if rstd_c is not None:
                c.op("vector", "scalar_tensor_tensor", b32(qby_y[p]),
                     b32(src2d), rstd_c, rsb, ALU.mult, ALU.mult).inc("v")
            else:
                c.op("vector", "tensor_tensor", b32(qby_y[p]), b32(src2d),
                     rsb, ALU.mult).inc("v")
            return c.cnt["v"]

        def q_yy(p, y_v):
            c.wait("scalar", "v", y_v)
            c.op("scalar", "activation", qb1, qby_y[p], AF.Square).inc("s")
            return c.cnt["s"]

        def q_B1(p, yy_s):
            c.wait("vector", "s", yy_s)
            c.op("vector", "tensor_scalar", qb2, qb1, 4.0, 0.125,
                 ALU.is_ge, ALU.mult).inc("v")
            c.op("vector", "tensor_scalar", rb_buf, qb1, 16.0, 0.25,
                 ALU.is_ge, ALU.mult).inc("v")
            c.wait("vector", "v")
            c.op("vector", "scalar_tensor_tensor", qb2, qb2, 0.125, rb_buf,
                 ALU.add, ALU.add).inc("v")
            c.wait("vector", "v")
            c.op("vector", "tensor_scalar", rb_buf, rb_buf, 24.0, 12.0,
                 ALU.mult, ALU.add).inc("v")
            c.wait("vector", "v")
            c.op("vector", "scalar_tensor_tensor", rb_buf, qb2, -32.0, rb_buf,
                 ALU.mult, ALU.add).inc("v")
            c.wait("vector", "v")
            c.op("vector", "tensor_tensor", qby_y[p], qby_y[p], rb_buf,
                 ALU.mult).inc("v")
            return c.cnt["v"]

        def q_cast(p, tq_v):
            c.wait("scalar", "v", tq_v)
            c.op("scalar", "copy", qbi, qby_y[p]).inc("s")
            return c.cnt["s"]

        def q_B2(p, cast_s, scratch, out_bf, xd_guard_t):
            c.wait("vector", "s", cast_s)
            c.wait("vector", "v")
            if xd_guard_t:
                c.wait("vector", "t", xd_guard_t)
            c.op("vector", "scalar_tensor_tensor", scratch, qbi, -1.0,
                 qby_y[p], ALU.mult, ALU.add).inc("v")
            c.wait("vector", "v")
            c.op("vector", "tensor_scalar", qby_y[p], scratch, 0.5, None,
                 ALU.is_gt).inc("v")
            c.op("vector", "tensor_scalar", cm_buf, scratch, -0.5, -1.0,
                 ALU.is_le, ALU.mult).inc("v")
            c.wait("vector", "v")
            c.op("vector", "scalar_tensor_tensor", scratch, qbi, 0.0,
                 qby_y[p], ALU.bypass, ALU.add).inc("v")
            c.wait("vector", "v")
            c.op("vector", "tensor_tensor", scratch, scratch, cm_buf,
                 ALU.add).inc("v")
            c.wait("vector", "v")
            c.op("vector", "tensor_tensor", scratch, scratch, qb2,
                 ALU.mult).inc("v")
            c.wait("vector", "v")
            sclb = q_scl2[p].unsqueeze(2).broadcast_to([128, 32, 32])
            c.op("vector", "tensor_tensor", b32(out_bf), b32(scratch), sclb,
                 ALU.mult).inc("v")
            return c.cnt["v"]

        def transpose_1024(src_bf, dst_fn, alt=False):
            t0 = c.cnt["t"]
            s0 = c.cnt["s"]
            v0 = c.cnt["v"]
            for kc in range(KC):
                pt = pt0 if kc % 2 == 0 else pt1
                if not alt:
                    c.wait("tensor", "s", s0 + kc - 1)
                elif kc % 2 == 0:
                    c.wait("tensor", "s", s0 + kc // 2)
                else:
                    c.wait("tensor", "v", v0 + (kc - 1) // 2)
                c.op("tensor", "transpose", pt[:],
                     src_bf[:, kc * 128:(kc + 1) * 128], id_sb).inc("t")
                if alt and kc % 2 == 1:
                    c.wait("vector", "t", t0 + kc + 1)
                    c.op("vector", "tensor_copy", dst_fn(kc), pt[:]).inc("v")
                else:
                    c.wait("scalar", "t", t0 + kc + 1)
                    c.op("scalar", "copy", dst_fn(kc), pt[:]).inc("s")
            return c.cnt["s"]

        def ffn_sublayer(build_wdt, emit_out=False):
            if not build_wdt:
                for rc in range(RC):
                    ln_chunk(x_sb, rc)
                ln_tail(EPS_LN)

            global qby_y, rb_buf, cm_buf
            if build_wdt:
                qby_y = [qby, qb3]
                rb_buf = qb0
                cm_buf = qb0
                wy = {}; wyy = {}; wtq = {}; wcast = {}; wxd = {}
                wtr = {}
                dma_here = {}

                def w_dma(jc):
                    st = wst_sb[jc % 2]
                    wsem = "wa" if jc % 2 == 0 else "wb"
                    if jc >= 2:
                        c.wait("sync", "v", wy[jc - 2])
                    c.wait("sync", wsem)
                    c.op("sync", "dma_start", out=st,
                         in_=w_dr["wf"][jc * 128:(jc + 1) * 128, :]).inc(wsem, 16)
                    dma_here[jc] = c.cnt[wsem]

                def w_B2(k):
                    xg = wtr[k - 2] if k >= 2 else 0
                    wxd[k] = q_B2(k % 2, wcast[k], ffntmp, wdrow_sb[k % 2], xg)

                def w_tr(k):
                    c.wait("tensor", "v", wxd[k])
                    transpose_1024(wdrow_sb[k % 2],
                                   lambda kc, jj=k: wdt_sb[:, kc,
                                                          jj * 128:(jj + 1) * 128])
                    wtr[k] = c.cnt["t"]

                for jc in range(2):
                    w_dma(jc)
                for jc in range(KC):
                    if _PARANOID:
                        c.fence()
                    wsem = "wa" if jc % 2 == 0 else "wb"
                    c.wait("vector", wsem, dma_here[jc])
                    wy[jc] = q_stageA(jc % 2, wst_sb[jc % 2], None)
                    wyy[jc] = q_yy(jc % 2, wy[jc])
                    if jc >= 1:
                        w_B2(jc - 1)
                    wtq[jc] = q_B1(jc % 2, wyy[jc])
                    wcast[jc] = q_cast(jc % 2, wtq[jc])
                    ln_chunk(x_sb, 2 * jc, x_ld[-1] + 16)
                    ln_chunk(x_sb, 2 * jc + 1, x_ld[-1] + 16)
                    if jc + 2 < KC:
                        w_dma(jc + 2)
                    if jc >= 1:
                        w_tr(jc - 1)
                w_B2(KC - 1)
                w_tr(KC - 1)
                ln_tail(EPS_LN)
                c.wait("gpsimd", "s")
                c.op("gpsimd", "dma_start", out=wdt_dr, in_=wdt_flat).inc("g", 16)
                w_end_v = c.cnt["v"]
            else:
                c.op("sync", "dma_start", out=wdt_flat, in_=wdt_dr).inc("ld", 16)
                c.wait("tensor", "ld")
                w_end_v = c.cnt["v"]

            qby_y = [qby, qb3]
            db = [qb0, wst_sb[1]]
            rb_buf = wst_sb[0]
            cm_buf = wst_sb[0]
            sd = {}; xy = {}; xyy = {}; xtq = {}; xcast = {}; xxd = {}
            tr_done = {}
            ev_done = {}
            mm_done = {}

            def x_d(rc, guard_v):
                c.wait("scalar", "v", guard_v)
                c.op("scalar", "activation", db[rc % 2], x_sb[:, rc, :],
                     AF.Identity, bias=negm_sb[:, rc:rc + 1]).inc("s")
                sd[rc] = c.cnt["s"]

            def x_B2(k):
                xg = tr_done[k - 2] if k >= 2 else 0
                xxd[k] = q_B2(k % 2, xcast[k], db[(k + 1) % 2],
                              xdrow_sb[k % 2], xg)

            def x_tr(k):
                c.wait("tensor", "v", xxd[k])
                s_after = transpose_1024(
                    xdrow_sb[k % 2],
                    lambda kc, rr=k: xdt_sb[:, kc, rr * 128:(rr + 1) * 128])
                tr_done[k] = c.cnt["t"]
                return s_after

            def x_mm(k, s_after):
                fp = psc[:, (k % 2) * 1024:(k % 2) * 1024 + 1024]
                c.wait("tensor", "s", s_after)
                if k >= 2:
                    c.wait("tensor", "v", ev_done[k - 2])
                for jh in range(2):
                    for kc in range(KC):
                        c.op("tensor", "matmul",
                             fp[:, jh * 512:(jh + 1) * 512],
                             xdt_sb[:, kc, k * 128:(k + 1) * 128],
                             wdt_sb[:, kc, jh * 512:(jh + 1) * 512],
                             start=(kc == 0), stop=False,
                             skip_group_check=True)
                    c.op("tensor", "matmul", fp[:, jh * 512:(jh + 1) * 512],
                         ones_row, bias_hi[0:1, jh * 512:(jh + 1) * 512],
                         start=False, stop=False, skip_group_check=True)
                    ins = c.op("tensor", "matmul", fp[:, jh * 512:(jh + 1) * 512],
                               ones_row, bias_lo[0:1, jh * 512:(jh + 1) * 512],
                               start=False, stop=True, skip_group_check=True)
                    if jh == 1:
                        ins.inc("t")
                mm_done[k] = c.cnt["t"]

            def ffn_evac(rc):
                fp = psc[:, (rc % 2) * 1024:(rc % 2) * 1024 + 1024]
                c.wait("vector", "t", mm_done[rc])
                c.wait("vector", "v")
                c.op("vector", "scalar_tensor_tensor", x_sb[:, rc, :], fp,
                     0.5, x_sb[:, rc, :], ALU.mult, ALU.add).inc("v")
                ev_done[rc] = c.cnt["v"]
                if emit_out:
                    c.wait("gpsimd", "v", ev_done[rc])
                    c.op("gpsimd", "dma_start", out=out_v[:, rc, :],
                         in_=x_sb[:, rc, :]).inc("g", 16)

            c.wait("vector", "ld")
            c.wait("tensor", "ld")
            c.wait("gpsimd", "ld")
            negm_v = c.cnt["v"]
            x_d(0, max(negm_v, w_end_v))
            x_d(1, max(negm_v, w_end_v))
            for rc in range(RC):
                if _PARANOID:
                    c.fence()
                c.wait("vector", "s", sd[rc])
                xy[rc] = q_stageA(rc % 2, db[rc % 2], ln_rstd[:, rc:rc + 1])
                xyy[rc] = q_yy(rc % 2, xy[rc])
                if rc >= 1:
                    x_B2(rc - 1)
                if rc >= 2:
                    ffn_evac(rc - 2)
                xtq[rc] = q_B1(rc % 2, xyy[rc])
                xcast[rc] = q_cast(rc % 2, xtq[rc])
                if rc + 2 < RC:
                    x_d(rc + 2, xxd[rc - 1] if rc >= 1 else xy[rc])
                if rc >= 1:
                    s_after = x_tr(rc - 1)
                    x_mm(rc - 1, s_after)
            x_B2(RC - 1)
            ffn_evac(RC - 2)
            s_after = x_tr(RC - 1)
            x_mm(RC - 1, s_after)
            ffn_evac(RC - 1)

        ffn_sublayer(build_wdt=True)
        c.fence()
        if stop_after == "ffn1":
            for i in range(4):
                c.op("gpsimd", "dma_start", out=out_v[:, 4 * i:4 * i + 4, :],
                     in_=x_sb[:, 4 * i:4 * i + 4, :]).inc("g", 16)
            c.wait("gpsimd", "g")
            return

        def store_from_spill():
            gsnap = c.cnt["g"]
            for rc in range(RC):
                if rc >= 2:
                    c.wait("gpsimd", "g", gsnap + rc * 32 - 16)
                c.op("gpsimd", "dma_start", out=xrow_sb[rc % 2],
                     in_=xsp_dr[:, rc * D:(rc + 1) * D]).inc("g", 16)
                c.wait("gpsimd", "g")
                c.op("gpsimd", "dma_start", out=out_v[:, rc, :],
                     in_=xrow_sb[rc % 2]).inc("g", 16)
            c.wait("gpsimd", "g")

        for rc in range(RC):
            ln_chunk(x_sb, rc)
        ln_tail(EPS_LN)
        if stop_after == "ln2":
            c.fence()
            for i in range(4):
                c.op("gpsimd", "dma_start", out=out_v[:, 4 * i:4 * i + 4, :],
                     in_=x_sb[:, 4 * i:4 * i + 4, :]).inc("g", 16)
            c.wait("gpsimd", "g")
            return
        t2tr = []
        for rc in range(RC):
            if rc >= 2:
                c.wait("vector", "t", t2tr[rc - 2])
            c.op("vector", "tensor_scalar", t2row_sb[rc % 2], x_sb[:, rc, :],
                 ln_mean[:, rc:rc + 1], ln_rstd[:, rc:rc + 1], ALU.subtract,
                 ALU.mult).inc("v")
            vhere = c.cnt["v"]
            c.wait("tensor", "v")
            transpose_1024(t2row_sb[rc % 2],
                           lambda kc, rr=rc: tt_sb[:, kc, rr * 128:(rr + 1) * 128],
                           alt=True)
            t2tr.append(c.cnt["t"])
            if stop_after != "t2nospill":
                c.wait("gpsimd", "v", vhere)
                c.op("gpsimd", "dma_start", out=xsp_dr[:, rc * D:(rc + 1) * D],
                     in_=x_sb[:, rc, :]).inc("g", 16)
        c.fence()
        if stop_after == "t2nospill":
            for i in range(4):
                c.op("gpsimd", "dma_start", out=out_v[:, 4 * i:4 * i + 4, :],
                     in_=x_sb[:, 4 * i:4 * i + 4, :]).inc("g", 16)
            c.wait("gpsimd", "g")
            return
        if stop_after == "t2":
            store_from_spill()
            return

        def load_weight(nm):
            wcast_done = []
            c.wait("vector", "t")
            c.wait("sync", "v")
            for kc in range(KC):
                st = wst_sb[kc % 2]
                wsem = "wa" if kc % 2 == 0 else "wb"
                if kc >= 2:
                    c.wait("sync", "v", wcast_done[kc - 2])
                c.wait("sync", wsem)
                c.op("sync", "dma_start", out=st,
                     in_=w_dr[nm][kc * 128:(kc + 1) * 128, :]).inc(wsem, 16)
                c.wait("vector", wsem)
                c.op("vector", "tensor_copy", wbf_sb[:, kc, :], st).inc("v")
                wcast_done.append(c.cnt["v"])
            c.wait("tensor", "v")

        def project_featmajor(dst, sc_dr):
            piece = []
            gam_done = []
            c.wait("sync", "v")
            c.wait("tensor", "v")
            c.wait("tensor", "s")
            for hc in range(H):
                gsem = "wa" if hc % 2 == 0 else "wb"
                if hc >= 2:
                    c.wait("sync", "v", gam_done[hc - 2])
                c.wait("sync", gsem)
                c.op("sync", "dma_start", out=gam_sb[hc % 2],
                     in_=sc_dr[hc:hc + 1, :].partition_broadcast(128)).inc(gsem, 16)
                ld_here = c.cnt[gsem]
                for nb in range(4):
                    i = len(piece)
                    half = psc[:, (i % 4) * 512:(i % 4) * 512 + 512]
                    if i >= 4:
                        c.wait("tensor", "v", piece[i - 4])
                    for kc in range(KC):
                        ins = c.op("tensor", "matmul", half,
                                   wbf_sb[:, kc, hc * 128:(hc + 1) * 128],
                                   tt_sb[:, kc, nb * 512:(nb + 1) * 512],
                                   start=(kc == 0), stop=(kc == KC - 1),
                                   skip_group_check=True)
                        if kc == KC - 1:
                            ins.inc("t")
                    c.wait("vector", "t")
                    c.wait("vector", gsem, ld_here)
                    c.op("vector", "tensor_tensor",
                         dst[:, hc, nb * 512:(nb + 1) * 512], half,
                         gam_sb[hc % 2][:, nb * 512:(nb + 1) * 512],
                         ALU.mult).inc("v")
                    piece.append(c.cnt["v"])
                gam_done.append(c.cnt["v"])

        def project_rowmajor(dst3d, act):
            piece = []
            c.wait("tensor", "v")
            c.wait("tensor", "s")
            for rc in range(RC):
                for db in range(2):
                    i = len(piece)
                    half = psc[:, (i % 4) * 512:(i % 4) * 512 + 512]
                    if i >= 4:
                        c.wait("tensor", "s", piece[i - 4])
                    for kc in range(KC):
                        ins = c.op("tensor", "matmul", half,
                                   tt_sb[:, kc, rc * 128:(rc + 1) * 128],
                                   wbf_sb[:, kc, db * 512:(db + 1) * 512],
                                   start=(kc == 0), stop=(kc == KC - 1),
                                   skip_group_check=True)
                        if kc == KC - 1:
                            ins.inc("t")
                    c.wait("scalar", "t")
                    if act is None:
                        c.op("scalar", "copy",
                             dst3d[:, rc, db * 512:(db + 1) * 512], half).inc("s")
                    else:
                        c.op("scalar", "activation",
                             dst3d[:, rc, db * 512:(db + 1) * 512], half,
                             act).inc("s")
                    piece.append(c.cnt["s"])

        load_weight("wq")
        project_featmajor(qt_sb, qsc_dr)
        load_weight("wk")
        project_featmajor(kt_sb, ksc_dr)
        load_weight("wv")
        project_rowmajor(v_sb, None)
        load_weight("wg")
        project_rowmajor(g_sb, AF.Silu)
        c.fence()
        if stop_after == "proj":
            store_from_spill()
            return

        piece_rel = [None, None, None, None]
        pp = 0
        gn_rel_s = 0
        gn_rel_v = 0
        ret_done = {}
        for hc in range(_NHEADS):
            for nh in range(2):
                MMAX = 8 * nh + 8

                def emit_ret(mc, ev_v, ev_s, sbuf):
                    c.wait("tensor", "v",
                           max(ev_v, gn_rel_v if mc == 0 else 0))
                    c.wait("tensor", "s",
                           max(ev_s, gn_rel_s if mc == 0 else 0))
                    for ncc in range(max(mc, 8 * nh), 8 * nh + 8):
                        n8 = ncc - 8 * nh
                        ins = c.op("tensor", "matmul",
                                   pbig[:, n8 * 128:(n8 + 1) * 128],
                                   sbuf[:, n8 * 128:(n8 + 1) * 128],
                                   v_sb[:, mc, hc * 128:(hc + 1) * 128],
                                   start=(mc == 0 and n8 % 4 == 0),
                                   stop=(mc == ncc),
                                   skip_group_check=True)
                    ins.inc("t")
                    ret_done[(hc, nh, mc)] = c.cnt["t"]

                pending = []
                evodd = 0
                for mc in range(MMAX):
                    nblo = max(2 * nh, mc // 4)
                    sbuf = strip_sb[mc % 4]
                    ev_v = ev_s = 0
                    has_diag = (mc // 4 >= 2 * nh)
                    prev_key = (hc, nh, mc - 4) if mc >= 4 else \
                        ((hc, nh - 1, MMAX - 8 + (mc - 4) % 8) if False else None)
                    strip_guard = ret_done.get((hc, nh, mc - 4), 0)
                    if mc < 4:
                        strip_guard = max(
                            [vv for kk, vv in ret_done.items()], default=0)
                    for nb in range(nblo, 2 * nh + 2):
                        slot = psc[:, (pp % 4) * 512:(pp % 4) * 512 + 512]
                        if piece_rel[pp % 4] is not None:
                            rsm, rvl = piece_rel[pp % 4]
                            c.wait("tensor", rsm, rvl)
                        c.op("tensor", "matmul", slot,
                             kt_sb[:, hc, mc * 128:(mc + 1) * 128],
                             qt_sb[:, hc, nb * 512:(nb + 1) * 512],
                             start=True, stop=True).inc("t")
                        there = max(c.cnt["t"], strip_guard)
                        nrel = nb * 512 - nh * 1024
                        if has_diag and nb == nblo:
                            o = mc * 128 - nb * 512
                            c.wait("vector", "t", there)
                            c.op("vector", "tensor_tensor",
                                 sbuf[:, mc * 128 - nh * 1024:
                                      mc * 128 - nh * 1024 + 128],
                                 slot[:, o:o + 128], m128_sb, ALU.mult).inc("v")
                            if o + 128 < 512:
                                c.wait("scalar", "v")
                                c.op("scalar", "copy",
                                     sbuf[:, nrel + o + 128:nrel + 512],
                                     slot[:, o + 128:512]).inc("s")
                                ev_s = c.cnt["s"]
                                piece_rel[pp % 4] = ("s", ev_s)
                            else:
                                ev_v = c.cnt["v"]
                                piece_rel[pp % 4] = ("v", ev_v)
                        else:
                            evodd += 1
                            if evodd % 2 == 0:
                                c.wait("vector", "t", there)
                                c.op("vector", "tensor_copy",
                                     sbuf[:, nrel:nrel + 512], slot).inc("v")
                                ev_v = max(ev_v, c.cnt["v"])
                                piece_rel[pp % 4] = ("v", c.cnt["v"])
                            else:
                                c.wait("scalar", "t", there)
                                c.op("scalar", "copy",
                                     sbuf[:, nrel:nrel + 512], slot).inc("s")
                                ev_s = max(ev_s, c.cnt["s"])
                                piece_rel[pp % 4] = ("s", c.cnt["s"])
                        pp += 1
                    if _HEAD_PARTS < 2:
                        continue
                    pending.append((mc, ev_v, ev_s, sbuf))
                    if len(pending) > 3:
                        emit_ret(*pending.pop(0))
                for args in pending:
                    emit_ret(*args)
                pending = []
                t_head = c.cnt["t"]
                if _HEAD_PARTS < 3:
                    continue
                c.wait("vector", "t", t_head)
                c.op("vector", "tensor_reduce",
                     gn_sum[:, 0:8].unsqueeze(2),
                     pbig[:].rearrange("p (a b) -> p a b", b=128), AX,
                     ALU.add).inc("v")
                red_done = c.cnt["v"]
                c.wait("scalar", "v", red_done)
                for n8 in range(8):
                    tile = pbig[:, n8 * 128:(n8 + 1) * 128]
                    if n8 >= 1:
                        c.wait("scalar", "s")
                    c.op("scalar", "activation", sqscr[:, 0:128], tile,
                         AF.Square, accum_out=gn_sq[:, n8:n8 + 1]).inc("s")
                if _HEAD_PARTS < 4:
                    continue
                c.wait("vector", "s")
                c.wait("vector", "v")
                gm = gn_mean[:, 0:8]
                gq = gn_sq[:, 0:8]
                gt = gn_tsc[:, 0:8]
                c.vsync(c.op("vector", "tensor_scalar", gm, gn_sum[:, 0:8],
                             1.0 / DH, None, ALU.mult))
                c.vsync(c.op("vector", "tensor_scalar", gq, gq, 1.0 / DH,
                             None, ALU.mult))
                c.vsync(c.op("vector", "scalar_tensor_tensor", gt, gm,
                             0.0, gm, ALU.bypass, ALU.mult))
                c.vsync(c.op("vector", "tensor_tensor", gq, gq, gt,
                             ALU.subtract))
                c.vsync(c.op("vector", "tensor_scalar", gq, gq, EPS_GN, None,
                             ALU.add))
                c.vsync(c.op("vector", "reciprocal", gt, gq))
                c.wait("scalar", "v")
                c.op("scalar", "activation", gn_rstd[:, 0:8], gt,
                     AF.Sqrt).inc("s")
                c.wait("vector", "s")
                if _HEAD_PARTS < 5:
                    continue
                b128 = lambda ap: ap.rearrange("p (a b) -> p a b", b=128)
                mb = gn_mean[:, 0:8].unsqueeze(2).broadcast_to([128, 8, 128])
                rbb = gn_rstd[:, 0:8].unsqueeze(2).broadcast_to([128, 8, 128])
                c.wait("vector", "s")
                c.wait("vector", "t")
                c.op("vector", "tensor_tensor", b128(yvf_hd),
                     pbig[:].rearrange("p (a b) -> p a b", b=128), mb,
                     ALU.subtract).inc("v")
                gn_rel_v = c.cnt["v"]
                c.wait("vector", "v")
                c.op("vector", "tensor_tensor", b128(yvf_hd), b128(yvf_hd),
                     rbb, ALU.mult).inc("v")
                c.wait("vector", "v")
                c.op("vector", "tensor_tensor", b128(zrowf_hd), b128(yvf_hd),
                     g_sb[:, 8 * nh:8 * nh + 8, hc * 128:(hc + 1) * 128],
                     ALU.mult).inc("v")
                zv = c.cnt["v"]
                t0z = c.cnt["t"]
                s0z = c.cnt["s"]
                for n8 in range(8):
                    ncc = 8 * nh + n8
                    pt = pt0 if n8 % 2 == 0 else pt1
                    c.wait("tensor", "v", zv)
                    c.wait("tensor", "s", s0z + n8 - 1)
                    c.op("tensor", "transpose", pt[:],
                         zrowf_hd[:, n8 * 128:(n8 + 1) * 128], id_sb).inc("t")
                    c.wait("scalar", "t", t0z + n8 + 1)
                    c.op("scalar", "copy",
                         zt_sb[:, hc, ncc * 128:(ncc + 1) * 128], pt[:]).inc("s")
                gn_rel_s = c.cnt["s"]
        c.fence()
        if stop_after == "heads":
            store_from_spill()
            return

        load_weight("wo")
        for rc in range(RC):
            rsem = "ga" if rc % 2 == 0 else "gb"
            if rc >= 2:
                c.wait("gpsimd", "v", wo_ev[rc - 2])
            c.wait("gpsimd", rsem)
            c.op("gpsimd", "dma_start", out=xrow_sb[rc % 2],
                 in_=xsp_dr[:, rc * D:(rc + 1) * D]).inc(rsem, 16)
            g_here = c.cnt[rsem]
            if rc >= 2:
                c.wait("tensor", "v", wo_ev[rc - 2])
            wp = psc[:, (rc % 2) * 1024:(rc % 2) * 1024 + 1024]
            for jh in range(2):
                for kc in range(H):
                    ins = c.op("tensor", "matmul",
                               wp[:, jh * 512:(jh + 1) * 512],
                               zt_sb[:, kc, rc * 128:(rc + 1) * 128],
                               wbf_sb[:, kc, jh * 512:(jh + 1) * 512],
                               start=(kc == 0), stop=(kc == H - 1),
                               skip_group_check=True)
                    if jh == 1 and kc == H - 1:
                        ins.inc("t")
            c.wait("vector", "t")
            c.wait("vector", rsem, g_here)
            c.op("vector", "tensor_tensor", x_sb[:, rc, :], xrow_sb[rc % 2],
                 wp, ALU.add).inc("v")
            wo_ev.append(c.cnt["v"])
        c.fence()
        if stop_after == "attn":
            for i in range(4):
                c.op("gpsimd", "dma_start", out=out_v[:, 4 * i:4 * i + 4, :],
                     in_=x_sb[:, 4 * i:4 * i + 4, :]).inc("g", 16)
            c.wait("gpsimd", "g")
            return

        ffn_sublayer(build_wdt=False, emit_out=True)
        c.wait("gpsimd", "g")

    with nc.Block() as block:
        @block.sync
        def _(e):
            emit(Ctx(nc, "sync", sems))

        @block.vector
        def _(e):
            emit(Ctx(nc, "vector", sems))

        @block.scalar
        def _(e):
            emit(Ctx(nc, "scalar", sems))

        @block.tensor
        def _(e):
            emit(Ctx(nc, "tensor", sems))

        @block.gpsimd
        def _(e):
            emit(Ctx(nc, "gpsimd", sems))

    sem_ctx.close()
    ctx_mem.close()
    return nc


def _host_consts():
    n = np.arange(S, dtype=np.float64)
    h = np.arange(H, dtype=np.float64)
    gam = 1.0 - np.exp2(-5.0 - h)
    qsc = ((gam[:, None] ** n[None, :]) / math.sqrt(DH)).astype(np.float32)
    ksc = (gam[:, None] ** (-n[None, :])).astype(np.float32)
    i = np.arange(128)
    m128 = (i[None, :] >= i[:, None]).astype(ml_dtypes.bfloat16)
    ident = np.eye(128, dtype=ml_dtypes.bfloat16)
    return qsc, ksc, m128, ident


def kernel(L0_attn_X, L0_attn_Wq, L0_attn_Wk, L0_attn_Wv, L0_attn_W_G,
           L0_attn_W_O, L0_ffn_weight, L0_ffn_bias):
    X = np.ascontiguousarray(np.asarray(L0_attn_X, dtype=np.float32))
    ws = {nm: np.ascontiguousarray(np.asarray(w, dtype=np.float32))
          for nm, w in (("wq", L0_attn_Wq), ("wk", L0_attn_Wk),
                        ("wv", L0_attn_Wv), ("wg", L0_attn_W_G),
                        ("wo", L0_attn_W_O), ("wf", L0_ffn_weight))}
    bias = np.ascontiguousarray(
        np.asarray(L0_ffn_bias, dtype=np.float32).reshape(1, D))
    qsc, ksc, m128, ident = _host_consts()

    nc = build_program(_STOP_AFTER)
    in_maps = []
    for b in range(8):
        m = dict(x=X[b], bias=bias, qsc=qsc, ksc=ksc, m128=m128, ident=ident)
        m.update(ws)
        in_maps.append(m)
    res = run_bass_kernel_spmd(nc, in_maps, core_ids=list(range(8)))
    out = np.stack([np.asarray(res.results[b]["out"]) for b in range(8)], 0)
    return out.astype(np.float32)


# version 21 (final)
# speedup vs baseline: 1.0651x; 1.0651x over previous
import math
import numpy as np
import ml_dtypes
import concourse.bass as bass
import concourse.mybir as mybir
from concourse.bass_utils import run_bass_kernel_spmd

F32 = mybir.dt.float32
BF16 = mybir.dt.bfloat16
I32 = mybir.dt.int32
U8 = mybir.dt.uint8
AF = mybir.ActivationFunctionType
ALU = mybir.AluOpType
AX = mybir.AxisListType.X

D = 1024
S = 2048
H = 8
DH = 128
RC = 16
KC = 8
EPS_LN = 1e-5
EPS_GN = 1e-6
R7_5 = float(np.float32(1.0) / np.float32(7.5))

_STOP_AFTER = "full"
_PARANOID = False
_NHEADS = 8
_HEAD_PARTS = 5

ENGS = ("sync", "vector", "scalar", "tensor", "gpsimd")
SEMS = ("ld", "g", "v", "s", "t", "wa", "wb", "ga", "gb")


class _I:
    __slots__ = ("ctx", "inst")

    def __init__(self, ctx, inst):
        self.ctx = ctx
        self.inst = inst

    def inc(self, sem, n=1):
        self.ctx.cnt[sem] += n
        if self.inst is not None:
            self.inst.then_inc(self.ctx.sems[sem], n)
        return self


class Ctx:

    def __init__(self, nc, cur, sems):
        self.nc = nc
        self.cur = cur
        self.sems = sems
        self.cnt = {k: 0 for k in SEMS}

    def op(self, eng, name, *a, **k):
        inst = None
        if self.cur == eng:
            inst = getattr(getattr(self.nc, eng), name)(*a, **k)
        return _I(self, inst)

    def wait(self, eng, sem, val=None):
        v = self.cnt[sem] if val is None else val
        if v <= 0:
            return
        if self.cur == eng:
            getattr(self.nc, eng).wait_ge(self.sems[sem], v)

    def fence(self):
        for e in ENGS:
            for sm in SEMS:
                self.wait(e, sm)

    def vsync(self, inst):
        inst.inc("v")
        self.wait("vector", "v")

    def ssync(self, inst):
        inst.inc("s")
        self.wait("scalar", "s")


def build_program(stop_after="full"):
    nc = bass.Bass("TRN2", target_bir_lowering=False)

    x_dr = nc.dram_tensor("x", [S, D], F32, kind="ExternalInput").ap()
    w_dr = {nm: nc.dram_tensor(nm, [D, D], F32, kind="ExternalInput").ap()
            for nm in ("wq", "wk", "wv", "wg", "wo", "wf")}
    bias_dr = nc.dram_tensor("bias", [1, D], F32, kind="ExternalInput").ap()
    qsc_dr = nc.dram_tensor("qsc", [H, S], F32, kind="ExternalInput").ap()
    ksc_dr = nc.dram_tensor("ksc", [H, S], F32, kind="ExternalInput").ap()
    m128_dr = nc.dram_tensor("m128", [128, 128], BF16, kind="ExternalInput").ap()
    id_dr = nc.dram_tensor("ident", [128, 128], BF16, kind="ExternalInput").ap()
    out_dr = nc.dram_tensor("out", [S, D], F32, kind="ExternalOutput").ap()
    xsp_dr = nc.dram_tensor("xspill", [128, RC * D], F32, kind="Internal").ap()
    wdt_dr = nc.dram_tensor("wdtspill", [128, KC * D], BF16, kind="Internal").ap()

    x_in = x_dr.rearrange("(rc p) d -> p rc d", p=128)
    out_v = out_dr.rearrange("(rc p) d -> p rc d", p=128)

    O_X = 0
    O_T = 65536
    O_V = 98304
    O_G = 131072
    O_W = 163840
    O_S = 188416
    O_C = 196608
    ARENA_B = 212480

    from contextlib import ExitStack
    ctx_mem = ExitStack()
    arena = ctx_mem.enter_context(nc.sbuf_tensor("arena", [128, ARENA_B], U8))
    pbig = ctx_mem.enter_context(nc.psum_tensor("pbig", [128, 1024], F32))
    psc = ctx_mem.enter_context(nc.psum_tensor("psc", [128, 2048], F32))
    pt0 = ctx_mem.enter_context(nc.psum_tensor("pt0", [128, 128], BF16))
    pt1 = ctx_mem.enter_context(nc.psum_tensor("pt1", [128, 128], BF16))

    A = arena[:]

    def vf32(off, n):
        return A[:, off:off + 4 * n].bitcast(F32)

    def vbf(off, n):
        return A[:, off:off + 2 * n].bitcast(BF16)

    def vi32(off, n):
        return A[:, off:off + 4 * n].bitcast(I32)

    x_sb = vf32(O_X, RC * D).rearrange("p (rc d) -> p rc d", d=D)
    qt_sb = vbf(O_X, H * S).rearrange("p (h n) -> p h n", n=S)
    kt_sb = vbf(O_X + 32768, H * S).rearrange("p (h n) -> p h n", n=S)
    tt_sb = vbf(O_T, KC * S).rearrange("p (k n) -> p k n", n=S)
    zt_sb = tt_sb
    v_sb = vbf(O_V, RC * D).rearrange("p (rc d) -> p rc d", d=D)
    xdt_sb = vbf(O_V, KC * S).rearrange("p (k n) -> p k n", n=S)
    g_sb = vbf(O_G, RC * D).rearrange("p (rc d) -> p rc d", d=D)
    wbf_sb = vbf(O_W, KC * D).rearrange("p (k d) -> p k d", d=D)
    wdt_sb = wbf_sb
    wdt_flat = vbf(O_W, KC * D)
    wst_sb = [vf32(O_W + 16384, D), vf32(O_W + 20480, D)]
    strip_sb = [vbf(O_S, 1024), vbf(O_S + 2048, 1024),
                vbf(O_S + 4096, 1024), vbf(O_S + 6144, 1024)]
    gam_sb = [vf32(O_G, S), vf32(O_G + 8192, S)]
    xrow_sb = [vf32(O_S, D), vf32(O_S + 4096, D)]
    t2row_sb = [vbf(O_S, D), vbf(O_S + 2048, D)]
    wdrow_sb = [vbf(O_S, D), vbf(O_S + 2048, D)]
    ffntmp = vf32(O_S + 4096, D)

    qb0 = vf32(O_G, D)
    qb1 = vf32(O_G + 4096, D)
    qb2 = vf32(O_G + 8192, D)
    qby = vf32(O_G + 12288, D)
    qbi = vi32(O_G + 16384, D)
    xdrow_sb = [vbf(O_G + 20480, D), vbf(O_G + 22528, D)]
    qb3 = vf32(O_G + 25600, D)
    qsm = vf32(O_G + 24576, 128)
    qsm2 = vf32(O_G + 25088, 128)
    q_amax2 = [qsm[:, 0:32], qsm2[:, 0:32]]
    q_scl2 = [qsm[:, 32:64], qsm2[:, 32:64]]
    q_rs2 = [qsm[:, 64:96], qsm2[:, 64:96]]

    bias_half = vf32(O_C, D)
    m128_sb = vbf(O_C + 4096, 128)
    id_sb = vbf(O_C + 4352, 128)
    stat = vf32(O_C + 4608, 144)
    ln_sum = stat[:, 0:16]
    ln_sq = stat[:, 16:32]
    ln_mean = stat[:, 32:48]
    ln_rstd = stat[:, 48:64]
    ln_tsc = stat[:, 64:80]
    gn_sum = stat[:, 80:96]
    gn_sq = stat[:, 96:112]
    gn_mean = stat[:, 112:128]
    gn_rstd = stat[:, 128:144]
    gn_tsc = vf32(O_C + 5184, 16)
    sqscr = vf32(O_C + 5248, D)
    yv_sb = [vf32(O_C + 9344, 128), vf32(O_C + 9856, 128)]
    zrow_sb = [vbf(O_C + 10368, 128), vbf(O_C + 10624, 128)]
    negm_sb = vf32(O_C + 10880, 16)
    ones_row = vbf(O_C + 10944, 128)[0:1, :]
    bias_hi = vbf(208384, D)
    bias_lo = vbf(210432, D)
    yvf_hd = vf32(O_W, D)
    zrowf_hd = vbf(O_W + 4096, D)

    sem_ctx = ExitStack()
    sems = {k: sem_ctx.enter_context(nc.semaphore(k)) for k in SEMS}

    def emit(c: Ctx):
        wq_src_done = []
        wq_tr_done = []
        wo_ev = []
        c.op("sync", "dma_start", out=bias_half,
             in_=bias_dr.partition_broadcast(128)).inc("wa", 16)
        c.op("sync", "dma_start", out=id_sb, in_=id_dr).inc("wb", 16)
        x_ld = []
        for i in range(4):
            c.op("sync", "dma_start", out=x_sb[:, 4 * i:4 * i + 4, :],
                 in_=x_in[:, 4 * i:4 * i + 4, :]).inc("ld", 16)
            x_ld.append(c.cnt["ld"])
        c.op("sync", "dma_start", out=m128_sb, in_=m128_dr).inc("ld", 16)
        c.op("vector", "memset", ones_row, 1.0).inc("v")
        c.wait("vector", "wa", 16)
        c.vsync(c.op("vector", "tensor_copy", bias_hi, bias_half))
        c.vsync(c.op("vector", "tensor_tensor", ffntmp, bias_half, bias_hi,
                     ALU.subtract))
        c.op("vector", "tensor_copy", bias_lo, ffntmp).inc("v")
        c.wait("tensor", "wb", 16)

        def ln_chunk(src3d, rc, gate_ld=None, gate_v=None):
            if gate_ld is not None:
                c.wait("scalar", "ld", gate_ld)
            if gate_v is not None:
                c.wait("scalar", "v", gate_v)
            c.wait("scalar", "s")
            c.op("scalar", "activation", sqscr, src3d[:, rc, :],
                 AF.Identity,
                 accum_out=ln_sum[:, rc:rc + 1]).inc("s")
            c.wait("scalar", "s")
            c.op("scalar", "activation", sqscr, src3d[:, rc, :], AF.Square,
                 accum_out=ln_sq[:, rc:rc + 1]).inc("s")

        def ln_tail(eps):
            c.wait("vector", "s")
            c.wait("vector", "v")
            c.vsync(c.op("vector", "tensor_scalar", ln_mean, ln_sum, 1.0 / D,
                         None, ALU.mult))
            c.vsync(c.op("vector", "tensor_scalar", ln_sq, ln_sq, 1.0 / D,
                         None, ALU.mult))
            c.vsync(c.op("vector", "scalar_tensor_tensor", ln_tsc, ln_mean,
                         0.0, ln_mean, ALU.bypass, ALU.mult))
            c.vsync(c.op("vector", "tensor_tensor", ln_sq, ln_sq, ln_tsc,
                         ALU.subtract))
            c.vsync(c.op("vector", "tensor_scalar", ln_sq, ln_sq, eps, None,
                         ALU.add))
            c.vsync(c.op("vector", "reciprocal", ln_tsc, ln_sq))
            c.wait("scalar", "v")
            c.op("scalar", "activation", ln_rstd, ln_tsc, AF.Sqrt).inc("s")
            c.wait("vector", "s")
            c.op("vector", "tensor_scalar", negm_sb[:, 0:16], ln_mean, -1.0,
                 None, ALU.mult).inc("v")

        b32 = lambda ap: ap.rearrange("p (a b) -> p a b", b=32)

        def q_stageA(p, src2d, rstd_c):
            amax, scl, rs = q_amax2[p], q_scl2[p], q_rs2[p]
            c.vsync(c.op("vector", "tensor_reduce", amax.unsqueeze(2),
                         b32(src2d), AX, ALU.max, apply_absolute_value=True))
            if rstd_c is not None:
                c.vsync(c.op("vector", "tensor_scalar", scl, amax, rstd_c,
                             None, ALU.mult))
                c.vsync(c.op("vector", "tensor_scalar", scl, scl, R7_5,
                             1e-30, ALU.mult, ALU.max))
            else:
                c.vsync(c.op("vector", "tensor_scalar", scl, amax, R7_5,
                             1e-30, ALU.mult, ALU.max))
            c.vsync(c.op("vector", "reciprocal", rs, scl))
            rsb = rs.unsqueeze(2).broadcast_to([128, 32, 32])
            if rstd_c is not None:
                c.op("vector", "scalar_tensor_tensor", b32(qby_y[p]),
                     b32(src2d), rstd_c, rsb, ALU.mult, ALU.mult).inc("v")
            else:
                c.op("vector", "tensor_tensor", b32(qby_y[p]), b32(src2d),
                     rsb, ALU.mult).inc("v")
            return c.cnt["v"]

        def q_yy(p, y_v):
            c.wait("scalar", "v", y_v)
            c.op("scalar", "activation", qb1, qby_y[p], AF.Square).inc("s")
            return c.cnt["s"]

        def q_B1(p, yy_s):
            c.wait("vector", "s", yy_s)
            c.op("vector", "tensor_scalar", qb2, qb1, 4.0, 0.125,
                 ALU.is_ge, ALU.mult).inc("v")
            c.op("vector", "tensor_scalar", rb_buf, qb1, 16.0, 0.25,
                 ALU.is_ge, ALU.mult).inc("v")
            c.wait("vector", "v")
            c.op("vector", "scalar_tensor_tensor", qb2, qb2, 0.125, rb_buf,
                 ALU.add, ALU.add).inc("v")
            c.wait("vector", "v")
            c.op("vector", "tensor_scalar", rb_buf, rb_buf, 24.0, 12.0,
                 ALU.mult, ALU.add).inc("v")
            c.wait("vector", "v")
            c.op("vector", "scalar_tensor_tensor", rb_buf, qb2, -32.0, rb_buf,
                 ALU.mult, ALU.add).inc("v")
            c.wait("vector", "v")
            c.op("vector", "tensor_tensor", qby_y[p], qby_y[p], rb_buf,
                 ALU.mult).inc("v")
            return c.cnt["v"]

        def q_cast(p, tq_v):
            c.wait("scalar", "v", tq_v)
            c.op("scalar", "copy", qbi, qby_y[p]).inc("s")
            return c.cnt["s"]

        def q_B2(p, cast_s, scratch, out_bf, xd_guard_t):
            c.wait("vector", "s", cast_s)
            c.wait("vector", "v")
            if xd_guard_t:
                c.wait("vector", "t", xd_guard_t)
            c.op("vector", "scalar_tensor_tensor", scratch, qbi, -1.0,
                 qby_y[p], ALU.mult, ALU.add).inc("v")
            c.wait("vector", "v")
            c.op("vector", "tensor_scalar", qby_y[p], scratch, 0.5, None,
                 ALU.is_gt).inc("v")
            c.op("vector", "tensor_scalar", cm_buf, scratch, -0.5, -1.0,
                 ALU.is_le, ALU.mult).inc("v")
            c.wait("vector", "v")
            c.op("vector", "scalar_tensor_tensor", scratch, qbi, 0.0,
                 qby_y[p], ALU.bypass, ALU.add).inc("v")
            c.wait("vector", "v")
            c.op("vector", "tensor_tensor", scratch, scratch, cm_buf,
                 ALU.add).inc("v")
            c.wait("vector", "v")
            c.op("vector", "tensor_tensor", scratch, scratch, qb2,
                 ALU.mult).inc("v")
            c.wait("vector", "v")
            sclb = q_scl2[p].unsqueeze(2).broadcast_to([128, 32, 32])
            c.op("vector", "tensor_tensor", b32(out_bf), b32(scratch), sclb,
                 ALU.mult).inc("v")
            return c.cnt["v"]

        def transpose_1024(src_bf, dst_fn):
            t0 = c.cnt["t"]
            s0 = c.cnt["s"]
            for kc in range(KC):
                pt = pt0 if kc % 2 == 0 else pt1
                c.wait("tensor", "s", s0 + kc - 1)
                c.op("tensor", "transpose", pt[:],
                     src_bf[:, kc * 128:(kc + 1) * 128], id_sb).inc("t")
                c.wait("scalar", "t", t0 + kc + 1)
                c.op("scalar", "copy", dst_fn(kc), pt[:]).inc("s")
            return c.cnt["s"]

        def ffn_sublayer(build_wdt, emit_out=False, ln_hoisted=False,
                         ln_next=False):
            if not build_wdt:
                if not ln_hoisted:
                    for rc in range(RC):
                        ln_chunk(x_sb, rc)
                ln_tail(EPS_LN)

            global qby_y, rb_buf, cm_buf
            if build_wdt:
                qby_y = [qby, qb3]
                rb_buf = qb0
                cm_buf = qb0
                wy = {}; wyy = {}; wtq = {}; wcast = {}; wxd = {}
                wtr = {}
                dma_here = {}

                def w_dma(jc):
                    st = wst_sb[jc % 2]
                    wsem = "wa" if jc % 2 == 0 else "wb"
                    if jc >= 2:
                        c.wait("sync", "v", wy[jc - 2])
                    c.wait("sync", wsem)
                    c.op("sync", "dma_start", out=st,
                         in_=w_dr["wf"][jc * 128:(jc + 1) * 128, :]).inc(wsem, 16)
                    dma_here[jc] = c.cnt[wsem]

                def w_B2(k):
                    xg = wtr[k - 2] if k >= 2 else 0
                    wxd[k] = q_B2(k % 2, wcast[k], ffntmp, wdrow_sb[k % 2], xg)

                def w_tr(k):
                    c.wait("tensor", "v", wxd[k])
                    transpose_1024(wdrow_sb[k % 2],
                                   lambda kc, jj=k: wdt_sb[:, kc,
                                                          jj * 128:(jj + 1) * 128])
                    wtr[k] = c.cnt["t"]

                for jc in range(2):
                    w_dma(jc)
                for jc in range(KC):
                    if _PARANOID:
                        c.fence()
                    wsem = "wa" if jc % 2 == 0 else "wb"
                    c.wait("vector", wsem, dma_here[jc])
                    wy[jc] = q_stageA(jc % 2, wst_sb[jc % 2], None)
                    wyy[jc] = q_yy(jc % 2, wy[jc])
                    if jc >= 1:
                        w_B2(jc - 1)
                    wtq[jc] = q_B1(jc % 2, wyy[jc])
                    wcast[jc] = q_cast(jc % 2, wtq[jc])
                    ln_chunk(x_sb, 2 * jc, x_ld[-1] + 16)
                    ln_chunk(x_sb, 2 * jc + 1, x_ld[-1] + 16)
                    if jc + 2 < KC:
                        w_dma(jc + 2)
                    if jc >= 1:
                        w_tr(jc - 1)
                w_B2(KC - 1)
                w_tr(KC - 1)
                ln_tail(EPS_LN)
                c.wait("gpsimd", "s")
                c.op("gpsimd", "dma_start", out=wdt_dr, in_=wdt_flat).inc("g", 16)
                w_end_v = c.cnt["v"]
            else:
                c.op("sync", "dma_start", out=wdt_flat, in_=wdt_dr).inc("ld", 16)
                c.wait("tensor", "ld")
                w_end_v = c.cnt["v"]

            qby_y = [qby, qb3]
            db = [qb0, wst_sb[1]]
            rb_buf = wst_sb[0]
            cm_buf = wst_sb[0]
            sd = {}; xy = {}; xyy = {}; xtq = {}; xcast = {}; xxd = {}
            tr_done = {}
            ev_done = {}
            mm_done = {}

            def x_d(rc, guard_v):
                c.wait("scalar", "v", guard_v)
                c.op("scalar", "activation", db[rc % 2], x_sb[:, rc, :],
                     AF.Identity, bias=negm_sb[:, rc:rc + 1]).inc("s")
                sd[rc] = c.cnt["s"]

            def x_B2(k):
                xg = tr_done[k - 2] if k >= 2 else 0
                xxd[k] = q_B2(k % 2, xcast[k], db[(k + 1) % 2],
                              xdrow_sb[k % 2], xg)

            def x_tr(k):
                c.wait("tensor", "v", xxd[k])
                s_after = transpose_1024(
                    xdrow_sb[k % 2],
                    lambda kc, rr=k: xdt_sb[:, kc, rr * 128:(rr + 1) * 128])
                tr_done[k] = c.cnt["t"]
                return s_after

            def x_mm(k, s_after):
                fp = psc[:, (k % 2) * 1024:(k % 2) * 1024 + 1024]
                c.wait("tensor", "s", s_after)
                if k >= 2:
                    c.wait("tensor", "v", ev_done[k - 2])
                for jh in range(2):
                    for kc in range(KC):
                        c.op("tensor", "matmul",
                             fp[:, jh * 512:(jh + 1) * 512],
                             xdt_sb[:, kc, k * 128:(k + 1) * 128],
                             wdt_sb[:, kc, jh * 512:(jh + 1) * 512],
                             start=(kc == 0), stop=False,
                             skip_group_check=True)
                    c.op("tensor", "matmul", fp[:, jh * 512:(jh + 1) * 512],
                         ones_row, bias_hi[0:1, jh * 512:(jh + 1) * 512],
                         start=False, stop=False, skip_group_check=True)
                    ins = c.op("tensor", "matmul", fp[:, jh * 512:(jh + 1) * 512],
                               ones_row, bias_lo[0:1, jh * 512:(jh + 1) * 512],
                               start=False, stop=True, skip_group_check=True)
                    if jh == 1:
                        ins.inc("t")
                mm_done[k] = c.cnt["t"]

            def ffn_evac(rc):
                fp = psc[:, (rc % 2) * 1024:(rc % 2) * 1024 + 1024]
                c.wait("vector", "t", mm_done[rc])
                c.wait("vector", "v")
                c.op("vector", "scalar_tensor_tensor", x_sb[:, rc, :], fp,
                     0.5, x_sb[:, rc, :], ALU.mult, ALU.add).inc("v")
                ev_done[rc] = c.cnt["v"]
                if emit_out:
                    c.wait("gpsimd", "v", ev_done[rc])
                    c.op("gpsimd", "dma_start", out=out_v[:, rc, :],
                         in_=x_sb[:, rc, :]).inc("g", 16)

            c.wait("vector", "ld")
            c.wait("tensor", "ld")
            c.wait("gpsimd", "ld")
            negm_v = c.cnt["v"]
            x_d(0, max(negm_v, w_end_v))
            x_d(1, max(negm_v, w_end_v))
            for rc in range(RC):
                if _PARANOID:
                    c.fence()
                c.wait("vector", "s", sd[rc])
                xy[rc] = q_stageA(rc % 2, db[rc % 2], ln_rstd[:, rc:rc + 1])
                xyy[rc] = q_yy(rc % 2, xy[rc])
                if rc >= 1:
                    x_B2(rc - 1)
                if rc >= 2:
                    ffn_evac(rc - 2)
                    if ln_next:
                        ln_chunk(x_sb, rc - 2, gate_v=ev_done[rc - 2])
                xtq[rc] = q_B1(rc % 2, xyy[rc])
                xcast[rc] = q_cast(rc % 2, xtq[rc])
                if rc + 2 < RC:
                    x_d(rc + 2, xxd[rc - 1] if rc >= 1 else xy[rc])
                if rc >= 1:
                    s_after = x_tr(rc - 1)
                    x_mm(rc - 1, s_after)
            x_B2(RC - 1)
            ffn_evac(RC - 2)
            s_after = x_tr(RC - 1)
            x_mm(RC - 1, s_after)
            ffn_evac(RC - 1)
            if ln_next:
                ln_chunk(x_sb, RC - 2, gate_v=ev_done[RC - 2])
                ln_chunk(x_sb, RC - 1, gate_v=ev_done[RC - 1])

        ffn_sublayer(build_wdt=True, ln_next=True)
        c.fence()
        if stop_after == "ffn1":
            for i in range(4):
                c.op("gpsimd", "dma_start", out=out_v[:, 4 * i:4 * i + 4, :],
                     in_=x_sb[:, 4 * i:4 * i + 4, :]).inc("g", 16)
            c.wait("gpsimd", "g")
            return

        def store_from_spill():
            gsnap = c.cnt["g"]
            for rc in range(RC):
                if rc >= 2:
                    c.wait("gpsimd", "g", gsnap + rc * 32 - 16)
                c.op("gpsimd", "dma_start", out=xrow_sb[rc % 2],
                     in_=xsp_dr[:, rc * D:(rc + 1) * D]).inc("g", 16)
                c.wait("gpsimd", "g")
                c.op("gpsimd", "dma_start", out=out_v[:, rc, :],
                     in_=xrow_sb[rc % 2]).inc("g", 16)
            c.wait("gpsimd", "g")

        ln_tail(EPS_LN)
        if stop_after == "ln2":
            c.fence()
            for i in range(4):
                c.op("gpsimd", "dma_start", out=out_v[:, 4 * i:4 * i + 4, :],
                     in_=x_sb[:, 4 * i:4 * i + 4, :]).inc("g", 16)
            c.wait("gpsimd", "g")
            return
        t2tr = []
        for rc in range(RC):
            if rc >= 2:
                c.wait("vector", "t", t2tr[rc - 2])
            c.op("vector", "tensor_scalar", t2row_sb[rc % 2], x_sb[:, rc, :],
                 ln_mean[:, rc:rc + 1], ln_rstd[:, rc:rc + 1], ALU.subtract,
                 ALU.mult).inc("v")
            vhere = c.cnt["v"]
            c.wait("tensor", "v")
            transpose_1024(t2row_sb[rc % 2],
                           lambda kc, rr=rc: tt_sb[:, kc, rr * 128:(rr + 1) * 128])
            t2tr.append(c.cnt["t"])
            if stop_after != "t2nospill":
                c.wait("gpsimd", "v", vhere)
                c.op("gpsimd", "dma_start", out=xsp_dr[:, rc * D:(rc + 1) * D],
                     in_=x_sb[:, rc, :]).inc("g", 16)
        c.fence()
        if stop_after == "t2nospill":
            for i in range(4):
                c.op("gpsimd", "dma_start", out=out_v[:, 4 * i:4 * i + 4, :],
                     in_=x_sb[:, 4 * i:4 * i + 4, :]).inc("g", 16)
            c.wait("gpsimd", "g")
            return
        if stop_after == "t2":
            store_from_spill()
            return

        def load_weight(nm):
            wcast_done = []
            c.wait("vector", "t")
            c.wait("sync", "v")
            for kc in range(KC):
                st = wst_sb[kc % 2]
                wsem = "wa" if kc % 2 == 0 else "wb"
                if kc >= 2:
                    c.wait("sync", "v", wcast_done[kc - 2])
                c.wait("sync", wsem)
                c.op("sync", "dma_start", out=st,
                     in_=w_dr[nm][kc * 128:(kc + 1) * 128, :]).inc(wsem, 16)
                c.wait("vector", wsem)
                c.op("vector", "tensor_copy", wbf_sb[:, kc, :], st).inc("v")
                wcast_done.append(c.cnt["v"])
            c.wait("tensor", "v")

        def project_featmajor(dst, sc_dr):
            piece = []
            gam_done = []
            c.wait("sync", "v")
            c.wait("tensor", "v")
            c.wait("tensor", "s")
            for hc in range(H):
                gsem = "wa" if hc % 2 == 0 else "wb"
                if hc >= 2:
                    c.wait("sync", "v", gam_done[hc - 2])
                c.wait("sync", gsem)
                c.op("sync", "dma_start", out=gam_sb[hc % 2],
                     in_=sc_dr[hc:hc + 1, :].partition_broadcast(128)).inc(gsem, 16)
                ld_here = c.cnt[gsem]
                for nb in range(4):
                    i = len(piece)
                    half = psc[:, (i % 4) * 512:(i % 4) * 512 + 512]
                    if i >= 4:
                        c.wait("tensor", "v", piece[i - 4])
                    for kc in range(KC):
                        ins = c.op("tensor", "matmul", half,
                                   wbf_sb[:, kc, hc * 128:(hc + 1) * 128],
                                   tt_sb[:, kc, nb * 512:(nb + 1) * 512],
                                   start=(kc == 0), stop=(kc == KC - 1),
                                   skip_group_check=True)
                        if kc == KC - 1:
                            ins.inc("t")
                    c.wait("vector", "t")
                    c.wait("vector", gsem, ld_here)
                    c.op("vector", "tensor_tensor",
                         dst[:, hc, nb * 512:(nb + 1) * 512], half,
                         gam_sb[hc % 2][:, nb * 512:(nb + 1) * 512],
                         ALU.mult).inc("v")
                    piece.append(c.cnt["v"])
                gam_done.append(c.cnt["v"])

        def project_rowmajor(dst3d, act):
            piece = []
            c.wait("tensor", "v")
            c.wait("tensor", "s")
            for rc in range(RC):
                for db in range(2):
                    i = len(piece)
                    half = psc[:, (i % 4) * 512:(i % 4) * 512 + 512]
                    if i >= 4:
                        c.wait("tensor", "s", piece[i - 4])
                    for kc in range(KC):
                        ins = c.op("tensor", "matmul", half,
                                   tt_sb[:, kc, rc * 128:(rc + 1) * 128],
                                   wbf_sb[:, kc, db * 512:(db + 1) * 512],
                                   start=(kc == 0), stop=(kc == KC - 1),
                                   skip_group_check=True)
                        if kc == KC - 1:
                            ins.inc("t")
                    c.wait("scalar", "t")
                    if act is None:
                        c.op("scalar", "copy",
                             dst3d[:, rc, db * 512:(db + 1) * 512], half).inc("s")
                    else:
                        c.op("scalar", "activation",
                             dst3d[:, rc, db * 512:(db + 1) * 512], half,
                             act).inc("s")
                    piece.append(c.cnt["s"])

        load_weight("wq")
        project_featmajor(qt_sb, qsc_dr)
        load_weight("wk")
        project_featmajor(kt_sb, ksc_dr)
        load_weight("wv")
        project_rowmajor(v_sb, None)
        load_weight("wg")
        project_rowmajor(g_sb, AF.Silu)
        c.fence()
        if stop_after == "proj":
            store_from_spill()
            return

        piece_rel = [None, None, None, None]
        pp = 0
        gn_rel_s = 0
        gn_rel_v = 0
        ret_done = {}
        for hc in range(_NHEADS):
            for nh in range(2):
                MMAX = 8 * nh + 8

                def emit_ret(mc, ev_v, ev_s, sbuf):
                    c.wait("tensor", "v",
                           max(ev_v, gn_rel_v if mc == 0 else 0))
                    c.wait("tensor", "s",
                           max(ev_s, gn_rel_s if mc == 0 else 0))
                    for ncc in range(max(mc, 8 * nh), 8 * nh + 8):
                        n8 = ncc - 8 * nh
                        ins = c.op("tensor", "matmul",
                                   pbig[:, n8 * 128:(n8 + 1) * 128],
                                   sbuf[:, n8 * 128:(n8 + 1) * 128],
                                   v_sb[:, mc, hc * 128:(hc + 1) * 128],
                                   start=(mc == 0 and n8 % 4 == 0),
                                   stop=(mc == ncc),
                                   skip_group_check=True)
                    ins.inc("t")
                    ret_done[(hc, nh, mc)] = c.cnt["t"]

                pending = []
                evodd = 0
                for mc in range(MMAX):
                    nblo = max(2 * nh, mc // 4)
                    sbuf = strip_sb[mc % 4]
                    ev_v = ev_s = 0
                    has_diag = (mc // 4 >= 2 * nh)
                    prev_key = (hc, nh, mc - 4) if mc >= 4 else \
                        ((hc, nh - 1, MMAX - 8 + (mc - 4) % 8) if False else None)
                    strip_guard = ret_done.get((hc, nh, mc - 4), 0)
                    if mc < 4:
                        strip_guard = max(
                            [vv for kk, vv in ret_done.items()], default=0)
                    for nb in range(nblo, 2 * nh + 2):
                        slot = psc[:, (pp % 4) * 512:(pp % 4) * 512 + 512]
                        if piece_rel[pp % 4] is not None:
                            rsm, rvl = piece_rel[pp % 4]
                            c.wait("tensor", rsm, rvl)
                        c.op("tensor", "matmul", slot,
                             kt_sb[:, hc, mc * 128:(mc + 1) * 128],
                             qt_sb[:, hc, nb * 512:(nb + 1) * 512],
                             start=True, stop=True).inc("t")
                        there = max(c.cnt["t"], strip_guard)
                        nrel = nb * 512 - nh * 1024
                        if has_diag and nb == nblo:
                            o = mc * 128 - nb * 512
                            c.wait("vector", "t", there)
                            c.op("vector", "tensor_tensor",
                                 sbuf[:, mc * 128 - nh * 1024:
                                      mc * 128 - nh * 1024 + 128],
                                 slot[:, o:o + 128], m128_sb, ALU.mult).inc("v")
                            if o + 128 < 512:
                                c.wait("scalar", "v")
                                c.op("scalar", "copy",
                                     sbuf[:, nrel + o + 128:nrel + 512],
                                     slot[:, o + 128:512]).inc("s")
                                ev_s = c.cnt["s"]
                                piece_rel[pp % 4] = ("s", ev_s)
                            else:
                                ev_v = c.cnt["v"]
                                piece_rel[pp % 4] = ("v", ev_v)
                        else:
                            evodd += 1
                            if evodd % 2 == 0:
                                c.wait("vector", "t", there)
                                c.op("vector", "tensor_copy",
                                     sbuf[:, nrel:nrel + 512], slot).inc("v")
                                ev_v = max(ev_v, c.cnt["v"])
                                piece_rel[pp % 4] = ("v", c.cnt["v"])
                            else:
                                c.wait("scalar", "t", there)
                                c.op("scalar", "copy",
                                     sbuf[:, nrel:nrel + 512], slot).inc("s")
                                ev_s = max(ev_s, c.cnt["s"])
                                piece_rel[pp % 4] = ("s", c.cnt["s"])
                        pp += 1
                    if _HEAD_PARTS < 2:
                        continue
                    pending.append((mc, ev_v, ev_s, sbuf))
                    if len(pending) > 3:
                        emit_ret(*pending.pop(0))
                for args in pending:
                    emit_ret(*args)
                pending = []
                t_head = c.cnt["t"]
                if _HEAD_PARTS < 3:
                    continue
                c.wait("vector", "t", t_head)
                c.op("vector", "tensor_reduce",
                     gn_sum[:, 0:8].unsqueeze(2),
                     pbig[:].rearrange("p (a b) -> p a b", b=128), AX,
                     ALU.add).inc("v")
                red_done = c.cnt["v"]
                c.wait("scalar", "v", red_done)
                for n8 in range(8):
                    tile = pbig[:, n8 * 128:(n8 + 1) * 128]
                    if n8 >= 1:
                        c.wait("scalar", "s")
                    c.op("scalar", "activation", sqscr[:, 0:128], tile,
                         AF.Square, accum_out=gn_sq[:, n8:n8 + 1]).inc("s")
                if _HEAD_PARTS < 4:
                    continue
                c.wait("vector", "s")
                c.wait("vector", "v")
                gm = gn_mean[:, 0:8]
                gq = gn_sq[:, 0:8]
                gt = gn_tsc[:, 0:8]
                c.vsync(c.op("vector", "tensor_scalar", gm, gn_sum[:, 0:8],
                             1.0 / DH, None, ALU.mult))
                c.vsync(c.op("vector", "tensor_scalar", gq, gq, 1.0 / DH,
                             None, ALU.mult))
                c.vsync(c.op("vector", "scalar_tensor_tensor", gt, gm,
                             0.0, gm, ALU.bypass, ALU.mult))
                c.vsync(c.op("vector", "tensor_tensor", gq, gq, gt,
                             ALU.subtract))
                c.vsync(c.op("vector", "tensor_scalar", gq, gq, EPS_GN, None,
                             ALU.add))
                c.vsync(c.op("vector", "reciprocal", gt, gq))
                c.wait("scalar", "v")
                c.op("scalar", "activation", gn_rstd[:, 0:8], gt,
                     AF.Sqrt).inc("s")
                c.wait("vector", "s")
                if _HEAD_PARTS < 5:
                    continue
                b128 = lambda ap: ap.rearrange("p (a b) -> p a b", b=128)
                mb = gn_mean[:, 0:8].unsqueeze(2).broadcast_to([128, 8, 128])
                rbb = gn_rstd[:, 0:8].unsqueeze(2).broadcast_to([128, 8, 128])
                c.wait("vector", "s")
                c.wait("vector", "t")
                c.op("vector", "tensor_tensor", b128(yvf_hd),
                     pbig[:].rearrange("p (a b) -> p a b", b=128), mb,
                     ALU.subtract).inc("v")
                gn_rel_v = c.cnt["v"]
                c.wait("vector", "v")
                c.op("vector", "tensor_tensor", b128(yvf_hd), b128(yvf_hd),
                     rbb, ALU.mult).inc("v")
                c.wait("vector", "v")
                c.op("vector", "tensor_tensor", b128(zrowf_hd), b128(yvf_hd),
                     g_sb[:, 8 * nh:8 * nh + 8, hc * 128:(hc + 1) * 128],
                     ALU.mult).inc("v")
                zv = c.cnt["v"]
                t0z = c.cnt["t"]
                s0z = c.cnt["s"]
                for n8 in range(8):
                    ncc = 8 * nh + n8
                    pt = pt0 if n8 % 2 == 0 else pt1
                    c.wait("tensor", "v", zv)
                    c.wait("tensor", "s", s0z + n8 - 1)
                    c.op("tensor", "transpose", pt[:],
                         zrowf_hd[:, n8 * 128:(n8 + 1) * 128], id_sb).inc("t")
                    c.wait("scalar", "t", t0z + n8 + 1)
                    c.op("scalar", "copy",
                         zt_sb[:, hc, ncc * 128:(ncc + 1) * 128], pt[:]).inc("s")
                gn_rel_s = c.cnt["s"]
        c.fence()
        if stop_after == "heads":
            store_from_spill()
            return

        load_weight("wo")
        for rc in range(RC):
            rsem = "ga" if rc % 2 == 0 else "gb"
            if rc >= 2:
                c.wait("gpsimd", "v", wo_ev[rc - 2])
            c.wait("gpsimd", rsem)
            c.op("gpsimd", "dma_start", out=xrow_sb[rc % 2],
                 in_=xsp_dr[:, rc * D:(rc + 1) * D]).inc(rsem, 16)
            g_here = c.cnt[rsem]
            if rc >= 2:
                c.wait("tensor", "v", wo_ev[rc - 2])
            wp = psc[:, (rc % 2) * 1024:(rc % 2) * 1024 + 1024]
            for jh in range(2):
                for kc in range(H):
                    ins = c.op("tensor", "matmul",
                               wp[:, jh * 512:(jh + 1) * 512],
                               zt_sb[:, kc, rc * 128:(rc + 1) * 128],
                               wbf_sb[:, kc, jh * 512:(jh + 1) * 512],
                               start=(kc == 0), stop=(kc == H - 1),
                               skip_group_check=True)
                    if jh == 1 and kc == H - 1:
                        ins.inc("t")
            c.wait("vector", "t")
            c.wait("vector", rsem, g_here)
            c.op("vector", "tensor_tensor", x_sb[:, rc, :], xrow_sb[rc % 2],
                 wp, ALU.add).inc("v")
            wo_ev.append(c.cnt["v"])
            ln_chunk(x_sb, rc, gate_v=wo_ev[-1])
        c.fence()
        if stop_after == "attn":
            for i in range(4):
                c.op("gpsimd", "dma_start", out=out_v[:, 4 * i:4 * i + 4, :],
                     in_=x_sb[:, 4 * i:4 * i + 4, :]).inc("g", 16)
            c.wait("gpsimd", "g")
            return

        ffn_sublayer(build_wdt=False, emit_out=True,
                     ln_hoisted=True)
        c.wait("gpsimd", "g")

    with nc.Block() as block:
        @block.sync
        def _(e):
            emit(Ctx(nc, "sync", sems))

        @block.vector
        def _(e):
            emit(Ctx(nc, "vector", sems))

        @block.scalar
        def _(e):
            emit(Ctx(nc, "scalar", sems))

        @block.tensor
        def _(e):
            emit(Ctx(nc, "tensor", sems))

        @block.gpsimd
        def _(e):
            emit(Ctx(nc, "gpsimd", sems))

    sem_ctx.close()
    ctx_mem.close()
    return nc


def _host_consts():
    n = np.arange(S, dtype=np.float64)
    h = np.arange(H, dtype=np.float64)
    gam = 1.0 - np.exp2(-5.0 - h)
    qsc = ((gam[:, None] ** n[None, :]) / math.sqrt(DH)).astype(np.float32)
    ksc = (gam[:, None] ** (-n[None, :])).astype(np.float32)
    i = np.arange(128)
    m128 = (i[None, :] >= i[:, None]).astype(ml_dtypes.bfloat16)
    ident = np.eye(128, dtype=ml_dtypes.bfloat16)
    return qsc, ksc, m128, ident


def kernel(L0_attn_X, L0_attn_Wq, L0_attn_Wk, L0_attn_Wv, L0_attn_W_G,
           L0_attn_W_O, L0_ffn_weight, L0_ffn_bias):
    X = np.ascontiguousarray(np.asarray(L0_attn_X, dtype=np.float32))
    ws = {nm: np.ascontiguousarray(np.asarray(w, dtype=np.float32))
          for nm, w in (("wq", L0_attn_Wq), ("wk", L0_attn_Wk),
                        ("wv", L0_attn_Wv), ("wg", L0_attn_W_G),
                        ("wo", L0_attn_W_O), ("wf", L0_ffn_weight))}
    bias = np.ascontiguousarray(
        np.asarray(L0_ffn_bias, dtype=np.float32).reshape(1, D))
    qsc, ksc, m128, ident = _host_consts()

    nc = build_program(_STOP_AFTER)
    in_maps = []
    for b in range(8):
        m = dict(x=X[b], bias=bias, qsc=qsc, ksc=ksc, m128=m128, ident=ident)
        m.update(ws)
        in_maps.append(m)
    res = run_bass_kernel_spmd(nc, in_maps, core_ids=list(range(8)))
    out = np.stack([np.asarray(res.results[b]["out"]) for b in range(8)], 0)
    return out.astype(np.float32)
